# version 23
# baseline (speedup 1.0000x reference)
import numpy as np
import ml_dtypes
import contextlib
import concourse.bass as bass
import concourse.mybir as mybir
from concourse.bass_utils import run_bass_kernel_spmd

F32 = mybir.dt.float32
BF16 = mybir.dt.bfloat16
ALU = mybir.AluOpType
ACTF = mybir.ActivationFunctionType
AX = mybir.AxisListType

D = 2048
NB = 16
H = 16
HK = 128
SEQ = 2
L = 2048
TPS = 4
SLAB = 512
NSLAB = L // SLAB
C = 32
NCH = SLAB // C
BIGS = float(2.0 ** 100)
RK = float(128.0 ** -0.5)
NCORES = 8


class Plan:
    def __init__(self, nc):
        self.nc = nc
        self.q = {e: [] for e in ("sync", "gpsimd", "tensor", "vector", "scalar")}
        self.cnt = {}
        self.sems = {}
        self.pools = {}
        self.last_wait = {}

    def reg(self, name, handle):
        self.sems[name] = handle
        self.cnt[name] = 0

    def reg_pool(self, name, handles):
        for i, h in enumerate(handles):
            self.reg(f"{name}{i}", h)
        self.pools[name] = {"n": len(handles), "uses": [0] * len(handles), "rr": 0}

    def wait(self, eng, sem, val=None):
        if val is None:
            val = self.cnt[sem]
        if val is None or val <= 0:
            return val
        key = (eng, sem)
        if self.last_wait.get(key, -1) >= val:
            return val
        self.last_wait[key] = val
        h = self.sems[sem]
        self.q[eng].append(lambda e, h=h, v=val: e.wait_ge(h, v))
        return val

    def wait_tok(self, eng, tok):
        if tok is None:
            return
        if isinstance(tok, list):
            for t in tok:
                self.wait_tok(eng, t)
            return
        self.wait(eng, tok[0], tok[1])

    def emit(self, eng, fn, inc=None, inc_by=1):
        if inc is None:
            self.q[eng].append(fn)
            return None
        h = self.sems[inc]
        def f(e, fn=fn, h=h, k=inc_by):
            fn(e).then_inc(h, k)
        self.q[eng].append(f)
        self.cnt[inc] += inc_by
        return self.cnt[inc]

    def dma(self, eng, out, in_, pool, slow=False):
        pl = self.pools[pool]
        slot = pl["rr"]
        pl["rr"] = (slot + 1) % pl["n"]
        sname = f"{pool}{slot}"
        u = pl["uses"][slot]
        if u > 0:
            self.wait(eng, sname, 16 * u)
        pl["uses"][slot] = u + 1
        if slow:
            self.emit(eng, lambda e, o=out, i=in_:
                      e.dma_start(out=o, in_=i, allow_slow_non_contiguous=True),
                      inc=sname, inc_by=16)
        else:
            self.emit(eng, lambda e, o=out, i=in_: e.dma_start(out=o, in_=i),
                      inc=sname, inc_by=16)
        return (sname, 16 * (u + 1))

    def pool_barrier_tok(self, pool):
        pl = self.pools[pool]
        return [(f"{pool}{i}", 16 * pl["uses"][i]) for i in range(pl["n"])
                if pl["uses"][i] > 0]


def build_program(SEQ=SEQ, L=L):
    NSLAB = L // SLAB
    nc = bass.Bass("TRN2", target_bir_lowering=False)

    dx = nc.dram_tensor("x", [SEQ * L, D], F32, kind="ExternalInput").ap()
    dwq = nc.dram_tensor("wq", [D, D], F32, kind="ExternalInput").ap()
    dwk = nc.dram_tensor("wk", [D, D], F32, kind="ExternalInput").ap()
    dwv = nc.dram_tensor("wv", [D, D], F32, kind="ExternalInput").ap()
    dwg = nc.dram_tensor("wg", [D, D], F32, kind="ExternalInput").ap()
    dwo = nc.dram_tensor("wo", [D, D], F32, kind="ExternalInput").ap()
    dfw = nc.dram_tensor("fw", [D, D], F32, kind="ExternalInput").ap()
    dgam = nc.dram_tensor("gamma", [D], F32, kind="ExternalInput").ap()
    dbet = nc.dram_tensor("beta", [D], F32, kind="ExternalInput").ap()
    dident = nc.dram_tensor("ident", [128, 128], BF16, kind="ExternalInput").ap()
    dbdm = nc.dram_tensor("bdmask", [128, 128], BF16, kind="ExternalInput").ap()
    dout = nc.dram_tensor("out", [SEQ * L, D], F32, kind="ExternalOutput").ap()

    s_qkg = nc.dram_tensor("s_qkg", [H, D, 3 * HK], BF16, kind="Internal").ap()
    s_v = nc.dram_tensor("s_v", [4, D, 512], BF16, kind="Internal").ap()
    s_o = nc.dram_tensor("s_o", [4, D, 512], BF16, kind="Internal").ap()
    s_f = nc.dram_tensor("s_f", [4, D, 512], BF16, kind="Internal").ap()

    p = Plan(nc)

    with contextlib.ExitStack() as ctx:
        def sb(name, shp, dt):
            return ctx.enter_context(nc.sbuf_tensor(name, shp, dt))

        def psum(name, shp, dt):
            return ctx.enter_context(nc.psum_tensor(name, shp, dt))

        x1A = [sb(f"x1a_{t}", [128, D], BF16) for t in range(TPS)]
        x1B = [sb(f"x1b_{t}", [128, D], BF16) for t in range(TPS)]
        xl2T = sb("xl2T", [128, NB * SLAB], BF16)
        vsl = [sb(f"v_{t}", [128, D], BF16) for t in range(TPS)]
        ysl = [sb(f"y_{t}", [128, D], BF16) for t in range(TPS)]
        Ssb = sb("S", [128, H * HK], BF16)
        fatw = [sb(f"fat{i}", [128, NB * 256], BF16) for i in range(2)]
        headw = [sb(f"hw{i}", [128, NB * 3 * HK], BF16) for i in range(2)]
        sg = sb("sg", [128, SLAB], F32)
        gb = sb("gb", [128, SLAB], F32)
        Dt = sb("Dt", [128, SLAB], F32)
        anch = sb("anch", [128, NCH], F32)
        lamf = sb("lamf", [128, H * NCH], F32)
        e_q = sb("e_q", [128, SLAB], BF16)
        e_k = sb("e_k", [128, SLAB], BF16)
        qt = sb("qt", [128, SLAB], BF16)
        kt = sb("kt", [128, SLAB], BF16)
        kh = sb("kh", [128, SLAB], BF16)
        khT = sb("khT", [128, SLAB], BF16)
        Asb = sb("Asb", [128, 512], BF16)
        shiftc = sb("shiftc", [128, 2 * TPS], F32)
        mnx = sb("mnx", [128, 2 * TPS], F32)
        zer = sb("zer", [128, SLAB], F32)
        qs1 = sb("qs1", [128, D], F32)
        F16 = mybir.dt.float16
        u0 = sb("u0", [128, D], F16)
        qr = sb("qr", [128, SLAB], BF16)
        kr = sb("kr", [128, SLAB], BF16)
        uw = sb("uw", [128, D], F16)
        bq = sb("bq", [128, D], F16)
        xq = sb("xq", [128, D], BF16)
        xqT = [sb(f"xqT{t}", [128, D], BF16) for t in range(TPS)]
        nt = sb("nt", [128, D], BF16)
        ost = sb("ost", [128, D], F32)
        st1 = sb("st1", [128, 2 * TPS], F32)
        st2 = sb("st2", [128, 2 * TPS], F32)
        mu = sb("mu", [128, 2 * TPS], F32)
        var = sb("var", [128, 2 * TPS], F32)
        rstd = sb("rstd", [128, 2 * TPS], F32)
        mtile = sb("mtile", [128, 2 * TPS], F32)
        beta_t = sb("beta_t", [128, 2 * TPS], F32)
        sfac = sb("sfac", [128, 2 * TPS], F32)
        combos = sb("combos", [128, 2 * TPS], F32)
        eps5 = sb("eps5", [128, 1], F32)
        one1 = sb("one1", [128, 1], F32)
        bias8 = sb("bias8", [128, 8], F32)
        alph = sb("alph", [128, 1], F32)
        thrP = sb("thrP", [128, 1], F32)
        thrN = sb("thrN", [128, 1], F32)
        onesf = sb("onesf", [128, 128], F32)
        gam = sb("gam", [128, NB], F32)
        bet = sb("bet", [128, NB], F32)
        ident = sb("identt", [128, 128], BF16)
        bdm = sb("bdm", [128, 128], BF16)
        accw = sb("accw", [128, 1], F32)

        pj = [psum(f"pj{i}", [128, 512], F32) for i in range(3)]
        pA = psum("pA", [128, 512], F32)
        pY = psum("pY", [128, 512], F32)
        pD = psum("pD", [128, 512], F32)
        pT = [psum(f"pT{i}", [128, 1024], BF16) for i in range(2)]

        for name in ("TD", "VD", "SD"):
            h = ctx.enter_context(nc.semaphore(name))
            p.reg(name, h)
        for name, nslots in (("LD", 8), ("WS", 8), ("GD", 8), ("OD", 4)):
            hs = [ctx.enter_context(nc.semaphore(f"{name}{i}")) for i in range(nslots)]
            p.reg_pool(name, hs)

        blk = ctx.enter_context(nc.Block())

        def v_op(meth, *args, inc="VD", **kw):
            p.wait("vector", "VD", p.cnt["VD"])
            return p.emit("vector", lambda e, a=args, k=kw: getattr(e, meth)(*a, **k), inc="VD")

        def s_op(meth, *args, inc="SD", **kw):
            p.wait("scalar", "SD", p.cnt["SD"])
            return p.emit("scalar", lambda e, a=args, k=kw: getattr(e, meth)(*a, **k), inc="SD")

        def t_mm(out, lhsT, rhs, start, stop, tp=None, inc=None, skip=False):
            return p.emit("tensor",
                          lambda e, o=out, l=lhsT, r=rhs, s0=start, s1=stop, t=tp, sk=skip:
                          e.matmul(o, l, r, start=s0, stop=s1, tile_position=t,
                                   skip_group_check=sk),
                          inc=inc)

        pT_last = {0: 0, 1: 0}
        tr_state = {"last_td": 0}

        def t_tr(bank, in_, inc="TD"):
            p.wait("tensor", "SD", pT_last[bank])
            td = p.emit("tensor",
                        lambda e, o=pT[bank][:, 0:128], i=in_: e.transpose(o, i, ident[:]),
                        inc=inc)
            tr_state["last_td"] = td
            return td

        def tile_stats(t, src_sb, ready_ld=None, minmax=False, cb=0, dump=None):
            t = cb + t
            if dump is None:
                dump = qs1
            if ready_ld is not None:
                p.wait_tok("vector", ready_ld)
                p.wait_tok("scalar", ready_ld)
            p.wait("scalar", "VD", p.cnt["VD"])
            v_op("tensor_reduce", st1[:, t:t + 1], src_sb[:], AX.X, ALU.add)
            s_op("activation", dump[:], src_sb[:], ACTF.Square, accum_out=st2[:, t:t + 1])
            if minmax:
                v_op("tensor_reduce", mtile[:, t:t + 1], src_sb[:], AX.X, ALU.max)
                v_op("tensor_reduce", mnx[:, t:t + 1], src_sb[:], AX.X, ALU.min)
            return p.cnt["VD"]

        def stats_finish(with_beta=False, lo=0, hi=TPS):
            def S(a):
                return a[:, lo:hi]
            p.wait("vector", "SD", p.cnt["SD"])
            v_op("tensor_scalar", S(mu), S(st1), 1.0 / D, None, ALU.mult, inc=None)
            v_op("tensor_scalar", S(var), S(st2), 1.0 / D, None, ALU.mult, inc=None)
            v_op("tensor_tensor", S(st1), S(mu), S(mu), ALU.mult, inc=None)
            vd = v_op("tensor_tensor", S(var), S(var), S(st1), ALU.subtract)
            p.wait("scalar", "VD", vd)
            s_op("activation", S(st2), S(var), ACTF.Ln, bias=eps5[:])
            sd = s_op("activation", S(rstd), S(st2), ACTF.Exp, scale=-0.5)
            p.wait("vector", "SD", sd)
            if not with_beta:
                return p.cnt["VD"]
            v_op("tensor_tensor", S(mtile), S(mtile), S(mu), ALU.subtract, inc=None)
            v_op("tensor_tensor", S(mnx), S(mu), S(mnx), ALU.subtract, inc=None)
            v_op("tensor_tensor", S(mtile), S(mtile), S(mnx), ALU.max, inc=None)
            v_op("tensor_tensor", S(mtile), S(mtile), S(rstd), ALU.mult, inc=None)
            v_op("tensor_scalar", S(beta_t), S(mtile), 1.0 / 127.0, 1e-8,
                 ALU.mult, ALU.max, inc=None)
            v_op("reciprocal", S(sfac), S(beta_t), inc=None)
            v_op("tensor_tensor", S(sfac), S(sfac), S(rstd), ALU.mult, inc=None)
            v_op("tensor_scalar", S(combos), S(beta_t), thrP[:], None, ALU.mult, inc=None)
            v_op("tensor_tensor", S(shiftc), S(mu), S(sfac), ALU.mult, inc=None)
            vd = v_op("tensor_scalar", S(shiftc), S(shiftc), -1.0, 128.5,
                      ALU.mult, ALU.add)
            return vd

        def quant_bits_gen(t, src_sb, out_sb, cb=0, full=False):
            tc = cb + t
            vd = v_op("tensor_scalar", u0[:], src_sb[:], sfac[:, tc:tc + 1],
                      shiftc[:, tc:tc + 1], ALU.mult, ALU.add)
            yield
            HW2 = D // 2
            halves = [slice(0, D)] if full else [slice(0, HW2), slice(HW2, D)]
            vd_h = [vd, vd]
            first = True
            for i, kb in enumerate(range(7, -1, -1)):
                src_u = u0 if first else uw
                if kb >= 3:
                    sd_h = [None, None]
                    for hv in range(len(halves)):
                        p.wait("scalar", "VD", vd_h[hv])
                        sd_h[hv] = s_op("activation", bq[:, halves[hv]],
                                        src_u[:, halves[hv]], ACTF.Sigmoid, scale=BIGS,
                                        bias=bias8[:, i:i + 1])
                    for hv in range(len(halves)):
                        p.wait("vector", "SD", sd_h[hv])
                        vd_h[hv] = v_op("scalar_tensor_tensor", uw[:, halves[hv]],
                                        bq[:, halves[hv]], -float(2 ** kb),
                                        src_u[:, halves[hv]], ALU.mult, ALU.add)
                    if full:
                        vd_h = [vd_h[0], vd_h[0]]
                else:
                    v_op("tensor_scalar", bq[:], src_u[:], float(2 ** kb), None,
                         ALU.is_ge, inc=None)
                    vdx = v_op("scalar_tensor_tensor", uw[:], bq[:], -float(2 ** kb),
                               src_u[:], ALU.mult, ALU.add)
                    vd_h = [vdx, vdx]
                first = False
                yield
            v_op("scalar_tensor_tensor", qs1[:], u0[:], -1.0, uw[:], ALU.mult, ALU.add,
                 inc=None)
            res[0] = v_op("tensor_scalar", out_sb[:], qs1[:], 1.0, 128.0,
                          ALU.mult, ALU.add)

        res = [0]

        def quant_bits(t, src_sb, out_sb=None, cb=0):
            if out_sb is None:
                out_sb = xq
            p.wait("vector", "TD", tr_state["last_td"])
            for _ in quant_bits_gen(t, src_sb, out_sb, cb):
                pass
            return res[0]

        def transpose_tile(src_sb, dest_fn, gamma_beta=False, src_ready_vd=None,
                           dest4_fn=None):
            if src_ready_vd is not None:
                p.wait("tensor", "VD", src_ready_vd)
            last_sd = 0
            for g4 in range(NB // 4):
                bank = g4 % 2
                p.wait("tensor", "SD", pT_last[bank])
                td = None
                for i in range(4):
                    ic = g4 * 4 + i
                    td = p.emit("tensor",
                                lambda e, o=pT[bank][:, 128 * i:128 * (i + 1)],
                                i_=src_sb[:, ic * 128:(ic + 1) * 128]:
                                e.transpose(o, i_, ident[:]),
                                inc=("TD" if i == 3 else None))
                tr_state["last_td"] = td
                p.wait("scalar", "TD", td)
                if dest4_fn is not None:
                    dst = dest4_fn(g4 * 4)
                else:
                    dst = dest_fn(g4 * 4)
                psrc = pT[bank][:, 0:512].rearrange("p (i c) -> p i c", c=128)
                if gamma_beta:
                    for i in range(4):
                        ic = g4 * 4 + i
                        sd = s_op("activation", dst[:, i, :], psrc[:, i, :],
                                  ACTF.Identity, bias=bet[:, ic:ic + 1],
                                  scale=gam[:, ic:ic + 1])
                else:
                    sd = s_op("copy", dst, psrc)
                pT_last[bank] = sd
                last_sd = sd
            return last_sd

        tok_c = [p.dma("gpsimd", ident[:], dident, "LD"),
                 p.dma("gpsimd", bdm[:], dbdm, "LD"),
                 p.dma("gpsimd", gam[:], dgam.rearrange("(c p) -> p c", p=128), "LD", slow=True),
                 p.dma("gpsimd", bet[:], dbet.rearrange("(c p) -> p c", p=128), "LD", slow=True)]
        p.wait_tok("vector", tok_c)
        v_op("memset", zer[:], 0.0)
        v_op("memset", eps5[:], 1e-5)
        v_op("memset", one1[:], 1.0)
        v_op("memset", accw[:], 0.0)
        v_op("memset", onesf[:], 1.0)
        v_op("memset", pA[:], 0.0)
        for i, kb in enumerate(range(7, -1, -1)):
            v_op("memset", bias8[:, i:i + 1], -BIGS * float(2 ** kb))
        vd_init = v_op("memset", anch[:, 0:1], 0.0)
        for eng in ("sync", "gpsimd", "tensor", "vector", "scalar"):
            p.wait(eng, "VD", vd_init)

        pbuf = [ost, qs1]
        lds = [p.dma("gpsimd", pbuf[r % 2][:], dfw[r * 128:(r + 1) * 128, :], "LD")
               for r in range(2)]
        vd = 0
        vd_red = {}
        for r in range(NB):
            p.wait_tok("vector", lds[r])
            vd_red[r] = v_op("tensor_reduce", st1[:, 0:1], pbuf[r % 2][:], AX.X, ALU.add,
                             apply_absolute_value=True)
            vd = v_op("tensor_tensor", accw[:], accw[:], st1[:, 0:1], ALU.add)
            if r + 2 < NB:
                p.wait("gpsimd", "VD", vd_red[r])
                lds.append(p.dma("gpsimd", pbuf[r % 2][:],
                                 dfw[(r + 2) * 128:(r + 3) * 128, :], "LD"))
        p.wait("tensor", "VD", vd)
        td = t_mm(pj[2][:, 0:1], onesf[:], accw[:], True, True, inc="TD")
        p.wait("vector", "TD", td)
        v_op("tensor_scalar", alph[:], pj[2][:, 0:1], 1.0 / (D * D), 1e-8,
             ALU.mult, ALU.max, inc=None)
        v_op("tensor_scalar", thrP[:], alph[:], 0.5, None, ALU.mult, inc=None)
        vd_alpha = v_op("tensor_scalar", thrN[:], alph[:], -0.5, None, ALU.mult)
        p.wait("scalar", "VD", vd_alpha)

        sf_toks = []
        stage = [ysl[0], ysl[1], ysl[2], ysl[3]]
        stage_tok = [[] for _ in range(4)]
        srcbuf = [ost, qs1]
        src_done = [("VD", 0), ("VD", 0)]
        tok_xq = []
        si = 0
        vd_tern = {}
        for r in range(NB):
            if r >= 2:
                p.wait("gpsimd", "VD", vd_tern[r - 2])
            else:
                p.wait("gpsimd", "VD", vd_red[14 + r])
            ld = p.dma("gpsimd", pbuf[r % 2][:], dfw[r * 128:(r + 1) * 128, :], "LD")
            p.wait_tok("vector", ld)
            tbuf = xq if r % 2 == 0 else nt
            p.wait_tok("vector", tok_xq[r % 2] if isinstance(tok_xq, dict) else None)
            if not isinstance(tok_xq, dict):
                tok_xq = {0: None, 1: None}
            p.wait_tok("vector", tok_xq[r % 2])
            v_op("tensor_scalar", bq[:], pbuf[r % 2][:], thrP[:], None, ALU.is_gt, inc=None)
            vd = v_op("scalar_tensor_tensor", tbuf[:], pbuf[r % 2][:], thrN[:], bq[:],
                      ALU.is_lt, ALU.subtract)
            vd_tern[r] = vd
            src_done[r % 2] = ("VD", vd)
            p.wait("sync", "VD", vd)
            tok_xq[r % 2] = p.dma("sync",
                                  s_f[:, r * 128:(r + 1) * 128, :].rearrange("g p c -> p g c"),
                                  tbuf[:].rearrange("p (g c) -> p g c", g=4), "WS")
            sf_toks.append(tok_xq[r % 2])
        xrows = dx.rearrange("(n p) d -> n p d", p=128)
        orows = dout.rearrange("(n p) d -> n p d", p=128)
        NSL = SEQ * NSLAB
        slabs = [x1A, x1B]

        def aq_slot(t):
            if t < 3:
                return headw[0][:, t * D:(t + 1) * D]
            return headw[1][:, 0:D]

        pref = {}
        od_slot_tok = {}
        last_c_vd = [0, 0]
        td_projlast_g = [0]

        def a_prefetch_gen(g):
            xsp = slabs[g % 2]
            lds = []
            for t in range(TPS):
                p.wait("gpsimd", "VD", last_c_vd[g % 2])
                lds.append(p.dma("gpsimd", xsp[t][:], xrows[g * TPS + t], "LD"))
            yield
            for t in range(TPS):
                tile_stats(t, xsp[t], ready_ld=lds[t], minmax=True, cb=4)
                yield
            stats_finish(with_beta=True, lo=4, hi=8)
            yield
            bits_vd = []
            for t in range(TPS):
                p.wait("vector", "TD", td_projlast_g[0])
                for _ in quant_bits_gen(t, xsp[t], aq_slot(t), cb=4):
                    yield
                bits_vd.append(res[0])
                yield
            pref[g] = bits_vd

        def pump(gen, n=1):
            if gen is None:
                return None
            for _ in range(n):
                if next(gen, "DONE") == "DONE":
                    return None
            return gen

        gen0 = a_prefetch_gen(0)
        while pump(gen0):
            pass
        src_done = [[("VD", p.cnt["VD"]), ("SD", p.cnt["SD"])],
                    [("VD", p.cnt["VD"]), ("SD", p.cnt["SD"])]]
        for r in range(NB):
            for mi, (msrc, mdst, scl, eng) in enumerate((
                    (dwv, s_v, None, "scalar"), (dwo, s_o, None, "scalar"),
                    (dwq, None, RK, "vector"), (dwk, None, None, "vector"),
                    (dwg, None, None, "vector"))):
                buf = stage[si % 4]
                sbuf_src = srcbuf[si % 2]
                p.wait_tok("gpsimd", [src_done[si % 2]])
                ld = p.dma("gpsimd", sbuf_src[:],
                           msrc[r * 128:(r + 1) * 128, :], "LD")
                p.wait_tok(eng, ld)
                p.wait_tok(eng, stage_tok[si % 4])
                stage_tok[si % 4] = []
                if eng == "scalar":
                    done = ("SD", s_op("copy", buf[:], sbuf_src[:]))
                elif scl is None:
                    done = ("VD", v_op("tensor_copy", buf[:], sbuf_src[:]))
                else:
                    done = ("VD", v_op("tensor_scalar", buf[:], sbuf_src[:], scl, None, ALU.mult))
                src_done[si % 2] = done
                p.wait_tok("sync", [done])
                if mdst is not None:
                    stage_tok[si % 4].append(
                        p.dma("sync",
                              mdst[:, r * 128:(r + 1) * 128, :].rearrange("g p c -> p g c"),
                              buf[:].rearrange("p (g c) -> p g c", g=4), "WS"))
                else:
                    j = mi - 2
                    stage_tok[si % 4].append(
                        p.dma("sync",
                              s_qkg[:, r * 128:(r + 1) * 128, j * HK:(j + 1) * HK]
                              .rearrange("h p c -> p h c"),
                              buf[:].rearrange("p (h c) -> p h c", h=H), "WS", slow=True))
                si += 1
        p.wait_tok("sync", sf_toks)
        p.wait_tok("vector", sf_toks)
        _gdb = p.pool_barrier_tok("WS")
        prep_done = {"armed": False}
        prep_stage_toks = [t for lst in stage_tok for t in lst]

        def load_piece(scratch, pg, buf):
            if scratch is not s_f and not prep_done["armed"]:
                prep_done["armed"] = True
                p.wait_tok("sync", _gdb)
                p.wait_tok("vector", _gdb)
                p.wait_tok("scalar", _gdb)
            og, half = pg // 2, pg % 2
            toks = []
            for part in range(2):
                toks.append(p.dma(
                    "sync",
                    fatw[buf][:].rearrange("p (ic c) -> p ic c", c=256)
                    [:, part * 8:(part + 1) * 8, :],
                    scratch[og, part * 1024:(part + 1) * 1024,
                            half * 256:(half + 1) * 256]
                    .rearrange("(ic p) c -> p ic c", p=128), "WS"))
            return toks

        def load_head(hh, buf):
            toks = []
            for part in range(2):
                toks.append(p.dma(
                    "sync",
                    headw[buf][:].rearrange("p (ic c) -> p ic c", c=3 * HK)
                    [:, part * 8:(part + 1) * 8, :],
                    s_qkg[hh, part * 1024:(part + 1) * 1024, :]
                    .rearrange("(ic p) c -> p ic c", p=128), "WS"))
            return toks

        def fat_phase(scratch, lhsT_fn, evac_fn, lhsT_ready_sd, pump_fn=None):
            if not isinstance(lhsT_ready_sd, list):
                lhsT_ready_sd = [lhsT_ready_sd] * TPS
            p.wait("sync", "TD", p.cnt["TD"])
            ws = [load_piece(scratch, 0, 0), load_piece(scratch, 1, 1)]
            evac_hist = {}
            td_piece = {}
            for pg in range(8):
                p.wait_tok("tensor", ws[pg])
                fb = fatw[pg % 2]
                for t in range(TPS):
                    p.wait("tensor", "SD", lhsT_ready_sd[t])
                    bank = pj[t % 2]
                    key = (pg * TPS + t) - 2
                    if key >= 0:
                        sname, sval = evac_hist[key]
                        p.wait("tensor", sname, sval)
                    td = None
                    for ic in range(NB):
                        td = t_mm(bank[:, 0:256], lhsT_fn(t, ic),
                                  fb[:, ic * 256:(ic + 1) * 256],
                                  ic == 0, ic == NB - 1,
                                  inc=("TD" if ic == NB - 1 else None))
                    evac_hist[pg * TPS + t] = evac_fn(t, pg, bank[:, 0:256], td)
                    if pump_fn is not None:
                        pump_fn()
                td_piece[pg] = td
                if pg + 2 < 8:
                    p.wait("sync", "TD", td_piece[pg])
                    ws.append(load_piece(scratch, pg + 2, pg % 2))
            for key in (8 * TPS - 2, 8 * TPS - 1):
                sname, sval = evac_hist[key]
                p.wait("tensor", sname, sval)
            return evac_hist

        def a_finish(gg, pump_fn=None):
            xsf = slabs[gg % 2]
            bits_vd = pref.pop(gg)
            trs = []
            for t in range(TPS):
                trs.append(transpose_tile(
                    aq_slot(t), None, src_ready_vd=bits_vd[t],
                    dest4_fn=lambda ic0, t=t: xqT[t][:, ic0 * 128:(ic0 + 4) * 128]
                    .rearrange("p (i c) -> p i c", c=128)))

            def evacA(t, pg, bank_ap, td):
                p.wait("vector", "TD", td)
                vd = v_op("scalar_tensor_tensor",
                          xsf[t][:, pg * 256:(pg + 1) * 256], bank_ap,
                          combos[:, 4 + t:5 + t],
                          xsf[t][:, pg * 256:(pg + 1) * 256], ALU.mult, ALU.add)
                return ("VD", vd)
            fat_phase(s_f, lambda t, ic: xqT[t][:, ic * 128:(ic + 1) * 128],
                      evacA, trs, pump_fn=pump_fn)

        for g in range(NSL):
            seq, sl = divmod(g, NSLAB)
            xs = slabs[g % 2]
            if sl == 0:
                vdS0 = v_op("memset", Ssb[:], 0.0)
                p.wait("tensor", "VD", vdS0)
                p.wait("vector", "VD", vdS0)

            if g == 0:
                a_finish(0, pump_fn=None)

            tr_sds = []
            for t in range(TPS):
                tile_stats(t, xs[t])
            stats_finish()
            for t in range(TPS):
                p.wait("vector", "TD", tr_state["last_td"])
                vd = v_op("tensor_scalar", nt[:], xs[t][:], mu[:, t:t + 1],
                          rstd[:, t:t + 1], ALU.subtract, ALU.mult)
                tr_sds.append(transpose_tile(
                    nt, None, src_ready_vd=vd,
                    dest4_fn=lambda ic0, t=t: xl2T[:]
                    .rearrange("p (ic s) -> p ic s", s=SLAB)
                    [:, ic0:ic0 + 4, t * 128:(t + 1) * 128]))

            def evacV(t, pg, bank_ap, td):
                p.wait("scalar", "TD", td)
                sd = s_op("copy", vsl[t][:, pg * 256:(pg + 1) * 256], bank_ap)
                return ("SD", sd)
            fat_phase(s_v,
                      lambda t, ic: xl2T[:, ic * SLAB + t * 128: ic * SLAB + (t + 1) * 128],
                      evacV, tr_sds)

            def qt_slot(h):
                return xqT[h // 4][:, (h % 4) * 512:(h % 4 + 1) * 512]

            def khT_slot(h, t):
                base2 = (h % 8) * 512 + t * 128
                return fatw[h // 8][:, base2:base2 + 128]

            p.wait("sync", "TD", p.cnt["TD"])
            ws_h = load_head(0, 0)

            def proj_qkg(h, gate_sd, q_sd, k_sd):
                hw = headw[h % 2]
                tds = {}
                for j, bank, wsd in ((2, pj[2], gate_sd),
                                     (0, pj[0], q_sd), (1, pj[1], k_sd)):
                    if wsd:
                        p.wait("tensor", "SD", wsd)
                    for ic in range(NB):
                        td = t_mm(bank[:],
                                  hw[:, ic * 384 + j * 128: ic * 384 + (j + 1) * 128],
                                  xl2T[:, ic * SLAB:(ic + 1) * SLAB],
                                  ic == 0, ic == NB - 1,
                                  inc=("TD" if ic == NB - 1 else None))
                    tds[j] = td
                return tds

            p.wait_tok("tensor", ws_h)
            if g == 0:
                p.wait_tok("scalar", prep_stage_toks)
                p.wait_tok("vector", prep_stage_toks)
            tds = proj_qkg(0, 0, None, None)
            td_kreaders = 0
            td_projlast = {-1: 0, 0: tds[1]}
            yevac_sd = 0
            for h in range(H):
                p.wait("scalar", "TD", tds[2])
                p.wait("scalar", "VD", p.cnt["VD"])
                sd_sig = s_op("copy", sg[:], pj[2][:])
                p.wait("scalar", "TD", tds[1])
                sd_q = s_op("copy", qr[:], pj[0][:])
                sd_k = s_op("copy", kr[:], pj[1][:])
                s_op("activation", gb[:], sg[:], ACTF.Exp)
                sd = s_op("activation", gb[:], gb[:], ACTF.Ln, bias=one1[:])
                p.wait("vector", "SD", sd)
                v_op("tensor_tensor", sg[:], sg[:], gb[:], ALU.subtract, inc=None)
                vd = v_op("tensor_tensor_scan", gb[:], sg[:], zer[:], 0.0,
                          ALU.add, ALU.add)
                gb3 = gb[:].rearrange("p (c j) -> p c j", j=C)
                v_op("tensor_copy", anch[:, 1:NCH].unsqueeze(2),
                     gb3[:, 0:NCH - 1, C - 1:C], inc=None)
                anb = anch[:].unsqueeze(2).broadcast_to([128, NCH, C])
                Dt3 = Dt[:].rearrange("p (c j) -> p c j", j=C)
                vd = v_op("tensor_tensor", Dt3, gb3, anb, ALU.subtract)
                p.wait("scalar", "VD", vd)
                sd_eq = s_op("activation", e_q[:], Dt[:], ACTF.Exp)
                sd_ek = s_op("activation", e_k[:], Dt[:], ACTF.Exp, scale=-1.0)
                Dend = Dt[:].rearrange("p (c j) -> p c j", j=C)[:, :, C - 1:C]
                sd_lam = s_op("activation",
                              lamf[:, h * NCH:(h + 1) * NCH].unsqueeze(2),
                              Dend, ACTF.Exp)
                p.wait("vector", "SD", sd_eq)
                q_vd = v_op("tensor_tensor", qt_slot(h), qr[:], e_q[:], ALU.mult)
                p.wait("vector", "SD", sd_ek)
                p.wait("vector", "TD", td_kreaders)
                k_vd = v_op("tensor_tensor", kt[:], kr[:], e_k[:], ALU.mult)
                lam3 = e_q[:].rearrange("p (c j) -> p c j", j=C)[:, :, C - 1:C] \
                    .broadcast_to([128, NCH, C])
                kt3 = kt[:].rearrange("p (c j) -> p c j", j=C)
                kh3 = kh[:].rearrange("p (c j) -> p c j", j=C)
                vd_kh = v_op("tensor_tensor", kh3, kt3, lam3, ALU.mult)
                p.wait("vector", "SD", sd_lam)
                vd_q = v_op("tensor_scalar", st2[:, 0:1], st1[:, 0:1], 1.0, None,
                            ALU.mult)
                if h + 1 < H:
                    p.wait("sync", "TD", td_projlast[h - 1])
                    ws_h = load_head(h + 1, (h + 1) % 2)
                    p.wait_tok("tensor", ws_h)
                    tds = proj_qkg(h + 1, sd_sig, sd_q, sd_k)
                    td_projlast[h + 1] = tds[1]
                p.wait("tensor", "VD", vd_q)
                bank = h % 2
                p.wait("tensor", "SD", pT_last[bank])
                for t in range(TPS):
                    td = p.emit("tensor",
                                lambda e, o=pT[bank][:, 128 * t:128 * (t + 1)],
                                i_=kh[:, t * 128:(t + 1) * 128]:
                                e.transpose(o, i_, ident[:]),
                                inc=("TD" if t == TPS - 1 else None))
                tr_state["last_td"] = td
                p.wait("scalar", "TD", td)
                sd_kh4 = s_op("copy",
                              fatw[h // 8][:, (h % 8) * 512:(h % 8 + 1) * 512]
                              .rearrange("p (i c) -> p i c", c=128),
                              pT[bank][:, 0:512].rearrange("p (i c) -> p i c", c=128))
                pT_last[bank] = sd_kh4
                for t in range(TPS):
                    tdA = t_mm(pA[:, 128 * t:128 * (t + 1)],
                               kt[:, t * 128:(t + 1) * 128],
                               qt_slot(h)[:, t * 128:(t + 1) * 128],
                               True, True, inc=("TD" if t == TPS - 1 else None))
                p.wait("vector", "TD", tdA)
                bdm4 = bdm[:].unsqueeze(1).broadcast_to([128, TPS, 128])
                vdA = v_op("tensor_tensor",
                           Asb[:].rearrange("p (t c) -> p t c", c=128),
                           pA[:, 0:512].rearrange("p (t c) -> p t c", c=128),
                           bdm4, ALU.mult)
                p.wait("tensor", "VD", vdA)
                p.wait("tensor", "SD", yevac_sd)
                for t in range(TPS):
                    p.wait("tensor", "SD", sd_kh4)
                    td_y = t_mm(pY[:, 128 * t:128 * (t + 1)],
                                Asb[:, 128 * t:128 * (t + 1)],
                                vsl[t][:, h * HK:(h + 1) * HK],
                                True, True, inc=("TD" if t == TPS - 1 else None))
                p.wait("scalar", "TD", td_y)
                for t in range(TPS):
                    yevac_sd = s_op("copy", ysl[t][:, h * HK:(h + 1) * HK],
                                    pY[:, 128 * t:128 * (t + 1)])
                td_kreaders = p.cnt["TD"]
            td_projlast_g[0] = td_projlast[H - 1]

            agen = a_prefetch_gen(g + 1) if g + 1 < NSL else None
            YB = [pj[2], pA, pY, pD]
            sd_stage1 = p.cnt["SD"]
            p.wait("tensor", "SD", sd_stage1)
            p.wait("tensor", "VD", p.cnt["VD"])
            vd_evac = [p.cnt["VD"]] * 4
            vdS_hist = {}
            for t in range(TPS):
                for j in range(TPS):
                    c = t * TPS + j
                    for hg in range(4):
                        bank = pj[hg % 2]
                        for hi in range(4):
                            h = hg * 4 + hi
                            gidx, gi = h // 4, h % 4
                            if c > 0:
                                p.wait("tensor", "VD", vdS_hist[(c - 1, h)])
                            if j == 0:
                                p.wait("tensor", "VD", vd_evac[gidx])
                            t_mm(YB[gidx][C * j:C * (j + 1), gi * 128:(gi + 1) * 128],
                                 qt_slot(h)[:, t * 128 + j * C: t * 128 + j * C + C],
                                 Ssb[:, h * HK:(h + 1) * HK],
                                 True, True, tp=(0, C * j), skip=True)
                        prevg = (c, hg - 2) if hg >= 2 else (c - 1, hg + 2)
                        if prevg in vdS_hist:
                            p.wait("tensor", "VD", vdS_hist[prevg])
                        tdS = None
                        for hi in range(4):
                            h = hg * 4 + hi
                            tdS = t_mm(bank[:, hi * 128:(hi + 1) * 128],
                                       khT_slot(h, t)[C * j:C * (j + 1), :],
                                       vsl[t][C * j:C * (j + 1), h * HK:(h + 1) * HK],
                                       True, True, tp=(C * j, 0),
                                       inc=("TD" if hi == 3 else None))
                        p.wait("vector", "TD", tdS)
                        S4 = Ssb[:, hg * 512:(hg + 1) * 512] \
                            .rearrange("p (i c2) -> p i c2", c2=128)
                        lam4 = lamf[:, hg * 4 * NCH + c: (hg * 4 + 3) * NCH + c + 1: NCH] \
                            .unsqueeze(2).broadcast_to([128, 4, 128])
                        v_op("tensor_tensor", S4, S4, lam4, ALU.mult, inc=None)
                        vd_s = v_op("tensor_tensor", S4, S4,
                                    bank[:, 0:512].rearrange("p (i c2) -> p i c2", c2=128),
                                    ALU.add)
                        for hi in range(4):
                            vdS_hist[(c, hg * 4 + hi)] = vd_s
                        vdS_hist[(c, hg)] = vd_s
                        agen = pump(agen)
                p.wait("vector", "TD", p.cnt["TD"])
                p.wait("vector", "SD", sd_stage1)
                for gg in range(4):
                    vd_evac[gg] = v_op("scalar_tensor_tensor",
                                       ysl[t][:, gg * 512:(gg + 1) * 512],
                                       YB[gg][:], 1.0,
                                       ysl[t][:, gg * 512:(gg + 1) * 512],
                                       ALU.mult, ALU.add)
                tile_stats(t, ysl[t], dump=nt)
            while pump(agen):
                pass

            tr_sds = []
            stats_finish()
            for t in range(TPS):
                p.wait("vector", "TD", tr_state["last_td"])
                vd = v_op("tensor_scalar", nt[:], ysl[t][:], mu[:, t:t + 1],
                          rstd[:, t:t + 1], ALU.subtract, ALU.mult)
                tr_sds.append(transpose_tile(
                    nt, None, gamma_beta=True, src_ready_vd=vd,
                    dest4_fn=lambda ic0, t=t: xqT[t][:, ic0 * 128:(ic0 + 4) * 128]
                    .rearrange("p (i c) -> p i c", c=128)))

            def evacO(t, pg, bank_ap, td):
                p.wait("vector", "TD", td)
                vd = v_op("scalar_tensor_tensor",
                          xs[t][:, pg * 256:(pg + 1) * 256], bank_ap, 1.0,
                          xs[t][:, pg * 256:(pg + 1) * 256], ALU.mult, ALU.add)
                return ("VD", vd)
            fat_phase(s_o, lambda t, ic: xqT[t][:, ic * 128:(ic + 1) * 128],
                      evacO, tr_sds)

            for t in range(TPS):
                tile_stats(t, xs[t], minmax=True)
            stats_finish(with_beta=True)

            cgens = [None]

            def c_steps():
                cvds = []
                for t in range(TPS):
                    p.wait("vector", "TD", tr_state["last_td"])
                    for _ in quant_bits_gen(t, xs[t], vsl[t]):
                        yield
                    cvds.append(res[0])
                    yield
                cgens[0] = cvds

            cg = c_steps()

            def cpump():
                next(cg, "DONE")
            if g + 1 < NSL:
                a_finish(g + 1, pump_fn=cpump)
            while next(cg, "DONE") != "DONE":
                pass
            cvds = cgens[0]
            tr_sds = []
            for t in range(TPS):
                p.wait("scalar", "TD", p.cnt["TD"])
                tr_sds.append(transpose_tile(
                    vsl[t], None, src_ready_vd=cvds[t],
                    dest4_fn=lambda ic0, t=t: xqT[t][:, ic0 * 128:(ic0 + 4) * 128]
                    .rearrange("p (i c) -> p i c", c=128)))

            ost8 = ost[:].rearrange("p (s c) -> p s c", c=256)

            def evacC(t, pg, bank_ap, td):
                slot = (pg * TPS + t) % 8
                p.wait("vector", "TD", td)
                p.wait_tok("vector", od_slot_tok.get(slot))
                vd = v_op("scalar_tensor_tensor",
                          ost8[:, slot, :], bank_ap, combos[:, t:t + 1],
                          xs[t][:, pg * 256:(pg + 1) * 256], ALU.mult, ALU.add)
                p.wait("gpsimd", "VD", vd)
                od_slot_tok[slot] = p.dma(
                    "gpsimd", orows[g * TPS + t][:, pg * 256:(pg + 1) * 256],
                    ost8[:, slot, :], "OD")
                return ("VD", vd)
            hist = fat_phase(s_f, lambda t, ic: xqT[t][:, ic * 128:(ic + 1) * 128],
                             evacC, tr_sds)
            last_c_vd[g % 2] = max(hist[7 * TPS + t][1] for t in range(TPS))

        p.wait_tok("sync", p.pool_barrier_tok("OD"))

        @blk.sync
        def _(e):
            for f in p.q["sync"]:
                f(e)

        @blk.gpsimd
        def _(e):
            for f in p.q["gpsimd"]:
                f(e)

        @blk.tensor
        def _(e):
            for f in p.q["tensor"]:
                f(e)

        @blk.vector
        def _(e):
            for f in p.q["vector"]:
                f(e)

        @blk.scalar
        def _(e):
            for f in p.q["scalar"]:
                f(e)

    return nc


_NC_CACHE = None


def kernel(L0_attn_x, L0_attn_Wq, L0_attn_Wk, L0_attn_Wv, L0_attn_Wg, L0_attn_Wo,
           L0_attn_gamma, L0_attn_beta, L0_ffn_weight):
    global _NC_CACHE
    if _NC_CACHE is None:
        _NC_CACHE = build_program()
    nc = _NC_CACHE

    x = np.ascontiguousarray(np.asarray(L0_attn_x, dtype=np.float32))
    wq = np.ascontiguousarray(np.asarray(L0_attn_Wq, np.float32).T)
    wk = np.ascontiguousarray(np.asarray(L0_attn_Wk, np.float32).T)
    wv = np.ascontiguousarray(np.asarray(L0_attn_Wv, np.float32).T)
    wg = np.ascontiguousarray(np.asarray(L0_attn_Wg, np.float32).T)
    wo = np.ascontiguousarray(np.asarray(L0_attn_Wo, np.float32).T)
    fw = np.ascontiguousarray(np.asarray(L0_ffn_weight, np.float32).T)
    gamma = np.ascontiguousarray(np.asarray(L0_attn_gamma, np.float32))
    beta = np.ascontiguousarray(np.asarray(L0_attn_beta, np.float32))
    ident = np.eye(128).astype(ml_dtypes.bfloat16)
    s_i = np.arange(128)[:, None]
    t_i = np.arange(128)[None, :]
    bdmask = (((s_i // C) == (t_i // C)) & (s_i <= t_i)).astype(ml_dtypes.bfloat16)

    in_maps = []
    for c in range(NCORES):
        shard = np.ascontiguousarray(x[2 * c:2 * c + 2].reshape(SEQ * L, D))
        in_maps.append({
            "x": shard, "wq": wq, "wk": wk, "wv": wv, "wg": wg, "wo": wo,
            "fw": fw, "gamma": gamma, "beta": beta,
            "ident": ident, "bdmask": bdmask,
        })
    res = run_bass_kernel_spmd(nc, in_maps, core_ids=list(range(NCORES)))
    out = np.empty((16, L, D), np.float32)
    for c in range(NCORES):
        out[2 * c:2 * c + 2] = np.asarray(res.results[c]["out"]).reshape(SEQ, L, D)
    return out


# version 24
# speedup vs baseline: 1.0465x; 1.0465x over previous
import numpy as np
import ml_dtypes
import contextlib
import concourse.bass as bass
import concourse.mybir as mybir
from concourse.bass_utils import run_bass_kernel_spmd

F32 = mybir.dt.float32
BF16 = mybir.dt.bfloat16
ALU = mybir.AluOpType
ACTF = mybir.ActivationFunctionType
AX = mybir.AxisListType

D = 2048
NB = 16
H = 16
HK = 128
SEQ = 2
L = 2048
TPS = 4
SLAB = 512
NSLAB = L // SLAB
C = 32
NCH = SLAB // C
BIGS = float(2.0 ** 100)
RK = float(128.0 ** -0.5)
NCORES = 8


class Plan:
    def __init__(self, nc):
        self.nc = nc
        self.q = {e: [] for e in ("sync", "gpsimd", "tensor", "vector", "scalar")}
        self.cnt = {}
        self.sems = {}
        self.pools = {}
        self.last_wait = {}

    def reg(self, name, handle):
        self.sems[name] = handle
        self.cnt[name] = 0

    def reg_pool(self, name, handles):
        for i, h in enumerate(handles):
            self.reg(f"{name}{i}", h)
        self.pools[name] = {"n": len(handles), "uses": [0] * len(handles), "rr": 0}

    def wait(self, eng, sem, val=None):
        if val is None:
            val = self.cnt[sem]
        if val is None or val <= 0:
            return val
        key = (eng, sem)
        if self.last_wait.get(key, -1) >= val:
            return val
        self.last_wait[key] = val
        h = self.sems[sem]
        self.q[eng].append(lambda e, h=h, v=val: e.wait_ge(h, v))
        return val

    def wait_tok(self, eng, tok):
        if tok is None:
            return
        if isinstance(tok, list):
            for t in tok:
                self.wait_tok(eng, t)
            return
        self.wait(eng, tok[0], tok[1])

    def emit(self, eng, fn, inc=None, inc_by=1):
        if inc is None:
            self.q[eng].append(fn)
            return None
        h = self.sems[inc]
        def f(e, fn=fn, h=h, k=inc_by):
            fn(e).then_inc(h, k)
        self.q[eng].append(f)
        self.cnt[inc] += inc_by
        return self.cnt[inc]

    def dma(self, eng, out, in_, pool, slow=False):
        pl = self.pools[pool]
        slot = pl["rr"]
        pl["rr"] = (slot + 1) % pl["n"]
        sname = f"{pool}{slot}"
        u = pl["uses"][slot]
        if u > 0:
            self.wait(eng, sname, 16 * u)
        pl["uses"][slot] = u + 1
        if slow:
            self.emit(eng, lambda e, o=out, i=in_:
                      e.dma_start(out=o, in_=i, allow_slow_non_contiguous=True),
                      inc=sname, inc_by=16)
        else:
            self.emit(eng, lambda e, o=out, i=in_: e.dma_start(out=o, in_=i),
                      inc=sname, inc_by=16)
        return (sname, 16 * (u + 1))

    def pool_barrier_tok(self, pool):
        pl = self.pools[pool]
        return [(f"{pool}{i}", 16 * pl["uses"][i]) for i in range(pl["n"])
                if pl["uses"][i] > 0]


def build_program(SEQ=SEQ, L=L):
    NSLAB = L // SLAB
    nc = bass.Bass("TRN2", target_bir_lowering=False)

    dx = nc.dram_tensor("x", [SEQ * L, D], F32, kind="ExternalInput").ap()
    dwq = nc.dram_tensor("wq", [D, D], F32, kind="ExternalInput").ap()
    dwk = nc.dram_tensor("wk", [D, D], F32, kind="ExternalInput").ap()
    dwv = nc.dram_tensor("wv", [D, D], F32, kind="ExternalInput").ap()
    dwg = nc.dram_tensor("wg", [D, D], F32, kind="ExternalInput").ap()
    dwo = nc.dram_tensor("wo", [D, D], F32, kind="ExternalInput").ap()
    dfw = nc.dram_tensor("fw", [D, D], F32, kind="ExternalInput").ap()
    dgam = nc.dram_tensor("gamma", [D], F32, kind="ExternalInput").ap()
    dbet = nc.dram_tensor("beta", [D], F32, kind="ExternalInput").ap()
    dident = nc.dram_tensor("ident", [128, 128], BF16, kind="ExternalInput").ap()
    dbdm = nc.dram_tensor("bdmask", [128, 128], BF16, kind="ExternalInput").ap()
    dout = nc.dram_tensor("out", [SEQ * L, D], F32, kind="ExternalOutput").ap()

    s_qkg = nc.dram_tensor("s_qkg", [H, D, 3 * HK], BF16, kind="Internal").ap()
    s_v = nc.dram_tensor("s_v", [4, D, 512], BF16, kind="Internal").ap()
    s_o = nc.dram_tensor("s_o", [4, D, 512], BF16, kind="Internal").ap()
    s_f = nc.dram_tensor("s_f", [4, D, 512], BF16, kind="Internal").ap()

    p = Plan(nc)

    with contextlib.ExitStack() as ctx:
        def sb(name, shp, dt):
            return ctx.enter_context(nc.sbuf_tensor(name, shp, dt))

        def psum(name, shp, dt):
            return ctx.enter_context(nc.psum_tensor(name, shp, dt))

        x1A = [sb(f"x1a_{t}", [128, D], BF16) for t in range(TPS)]
        x1B = [sb(f"x1b_{t}", [128, D], BF16) for t in range(TPS)]
        xl2T = sb("xl2T", [128, NB * SLAB], BF16)
        vsl = [sb(f"v_{t}", [128, D], BF16) for t in range(TPS)]
        ysl_all = sb("y_all", [128, TPS * D], BF16)
        ysl = [ysl_all[:, t * D:(t + 1) * D] for t in range(TPS)]
        Ssb = sb("S", [128, H * HK], BF16)
        fatw = [sb(f"fat{i}", [128, NB * 256], BF16) for i in range(2)]
        headw = [sb(f"hw{i}", [128, NB * 3 * HK], BF16) for i in range(2)]
        sg = sb("sg", [128, SLAB], F32)
        gb = sb("gb", [128, SLAB], F32)
        Dt = sb("Dt", [128, SLAB], F32)
        anch = sb("anch", [128, NCH], F32)
        lamf = sb("lamf", [128, H * NCH], F32)
        e_q = sb("e_q", [128, SLAB], BF16)
        e_k = sb("e_k", [128, SLAB], BF16)
        qt = sb("qt", [128, SLAB], BF16)
        kt = sb("kt", [128, SLAB], BF16)
        kh = sb("kh", [128, SLAB], BF16)
        khT = sb("khT", [128, SLAB], BF16)
        Asb = sb("Asb", [128, 512], BF16)
        shiftc = sb("shiftc", [128, 2 * TPS], F32)
        mnx = sb("mnx", [128, 2 * TPS], F32)
        zer = sb("zer", [128, SLAB], F32)
        qs1 = sb("qs1", [128, D], F32)
        F16 = mybir.dt.float16
        u0 = sb("u0", [128, D], F16)
        qr = sb("qr", [128, SLAB], BF16)
        kr = sb("kr", [128, SLAB], BF16)
        uw = sb("uw", [128, D], F16)
        bq = sb("bq", [128, D], F16)
        xq = sb("xq", [128, D], BF16)
        xqT = [sb(f"xqT{t}", [128, D], BF16) for t in range(TPS)]
        nt = sb("nt", [128, D], BF16)
        ost = sb("ost", [128, D], F32)
        st1 = sb("st1", [128, 2 * TPS], F32)
        st2 = sb("st2", [128, 2 * TPS], F32)
        mu = sb("mu", [128, 2 * TPS], F32)
        var = sb("var", [128, 2 * TPS], F32)
        rstd = sb("rstd", [128, 2 * TPS], F32)
        mtile = sb("mtile", [128, 2 * TPS], F32)
        beta_t = sb("beta_t", [128, 2 * TPS], F32)
        sfac = sb("sfac", [128, 2 * TPS], F32)
        combos = sb("combos", [128, 2 * TPS], F32)
        eps5 = sb("eps5", [128, 1], F32)
        one1 = sb("one1", [128, 1], F32)
        bias8 = sb("bias8", [128, 8], F32)
        alph = sb("alph", [128, 1], F32)
        thrP = sb("thrP", [128, 1], F32)
        thrN = sb("thrN", [128, 1], F32)
        onesf = sb("onesf", [128, 128], F32)
        gam = sb("gam", [128, NB], F32)
        bet = sb("bet", [128, NB], F32)
        ident = sb("identt", [128, 128], BF16)
        bdm = sb("bdm", [128, 128], BF16)
        accw = sb("accw", [128, 1], F32)

        pj = [psum(f"pj{i}", [128, 512], F32) for i in range(3)]
        pA = psum("pA", [128, 512], F32)
        pY = psum("pY", [128, 512], F32)
        pD = psum("pD", [128, 512], F32)
        pT = [psum(f"pT{i}", [128, 1024], BF16) for i in range(2)]

        for name in ("TD", "VD", "SD"):
            h = ctx.enter_context(nc.semaphore(name))
            p.reg(name, h)
        for name, nslots in (("LD", 8), ("WS", 8), ("GD", 8), ("OD", 4)):
            hs = [ctx.enter_context(nc.semaphore(f"{name}{i}")) for i in range(nslots)]
            p.reg_pool(name, hs)

        blk = ctx.enter_context(nc.Block())

        def v_op(meth, *args, inc="VD", **kw):
            p.wait("vector", "VD", p.cnt["VD"])
            return p.emit("vector", lambda e, a=args, k=kw: getattr(e, meth)(*a, **k), inc="VD")

        def s_op(meth, *args, inc="SD", **kw):
            p.wait("scalar", "SD", p.cnt["SD"])
            return p.emit("scalar", lambda e, a=args, k=kw: getattr(e, meth)(*a, **k), inc="SD")

        def t_mm(out, lhsT, rhs, start, stop, tp=None, inc=None, skip=False):
            return p.emit("tensor",
                          lambda e, o=out, l=lhsT, r=rhs, s0=start, s1=stop, t=tp, sk=skip:
                          e.matmul(o, l, r, start=s0, stop=s1, tile_position=t,
                                   skip_group_check=sk),
                          inc=inc)

        pT_last = {0: 0, 1: 0}
        tr_state = {"last_td": 0}

        def t_tr(bank, in_, inc="TD"):
            p.wait("tensor", "SD", pT_last[bank])
            td = p.emit("tensor",
                        lambda e, o=pT[bank][:, 0:128], i=in_: e.transpose(o, i, ident[:]),
                        inc=inc)
            tr_state["last_td"] = td
            return td

        def tile_stats(t, src_sb, ready_ld=None, minmax=False, cb=0, dump=None):
            t = cb + t
            if dump is None:
                dump = qs1
            if ready_ld is not None:
                p.wait_tok("vector", ready_ld)
                p.wait_tok("scalar", ready_ld)
            p.wait("scalar", "VD", p.cnt["VD"])
            v_op("tensor_reduce", st1[:, t:t + 1], src_sb[:], AX.X, ALU.add)
            s_op("activation", dump[:], src_sb[:], ACTF.Square, accum_out=st2[:, t:t + 1])
            if minmax:
                v_op("tensor_reduce", mtile[:, t:t + 1], src_sb[:], AX.X, ALU.max)
                v_op("tensor_reduce", mnx[:, t:t + 1], src_sb[:], AX.X, ALU.min)
            return p.cnt["VD"]

        def stats_finish(with_beta=False, lo=0, hi=TPS):
            def S(a):
                return a[:, lo:hi]
            p.wait("vector", "SD", p.cnt["SD"])
            v_op("tensor_scalar", S(mu), S(st1), 1.0 / D, None, ALU.mult, inc=None)
            v_op("tensor_scalar", S(var), S(st2), 1.0 / D, None, ALU.mult, inc=None)
            v_op("tensor_tensor", S(st1), S(mu), S(mu), ALU.mult, inc=None)
            vd = v_op("tensor_tensor", S(var), S(var), S(st1), ALU.subtract)
            p.wait("scalar", "VD", vd)
            s_op("activation", S(st2), S(var), ACTF.Ln, bias=eps5[:])
            sd = s_op("activation", S(rstd), S(st2), ACTF.Exp, scale=-0.5)
            p.wait("vector", "SD", sd)
            if not with_beta:
                return p.cnt["VD"]
            v_op("tensor_tensor", S(mtile), S(mtile), S(mu), ALU.subtract, inc=None)
            v_op("tensor_tensor", S(mnx), S(mu), S(mnx), ALU.subtract, inc=None)
            v_op("tensor_tensor", S(mtile), S(mtile), S(mnx), ALU.max, inc=None)
            v_op("tensor_tensor", S(mtile), S(mtile), S(rstd), ALU.mult, inc=None)
            v_op("tensor_scalar", S(beta_t), S(mtile), 1.0 / 127.0, 1e-8,
                 ALU.mult, ALU.max, inc=None)
            v_op("reciprocal", S(sfac), S(beta_t), inc=None)
            v_op("tensor_tensor", S(sfac), S(sfac), S(rstd), ALU.mult, inc=None)
            v_op("tensor_scalar", S(combos), S(beta_t), thrP[:], None, ALU.mult, inc=None)
            v_op("tensor_tensor", S(shiftc), S(mu), S(sfac), ALU.mult, inc=None)
            vd = v_op("tensor_scalar", S(shiftc), S(shiftc), -1.0, 128.5,
                      ALU.mult, ALU.add)
            return vd

        def quant_bits_gen(t, src_sb, out_sb, cb=0, full=False):
            tc = cb + t
            vd = v_op("tensor_scalar", u0[:], src_sb[:], sfac[:, tc:tc + 1],
                      shiftc[:, tc:tc + 1], ALU.mult, ALU.add)
            yield
            HW2 = D // 2
            halves = [slice(0, D)] if full else [slice(0, HW2), slice(HW2, D)]
            vd_h = [vd, vd]
            first = True
            for i, kb in enumerate(range(7, -1, -1)):
                src_u = u0 if first else uw
                if kb >= 3:
                    sd_h = [None, None]
                    for hv in range(len(halves)):
                        p.wait("scalar", "VD", vd_h[hv])
                        sd_h[hv] = s_op("activation", bq[:, halves[hv]],
                                        src_u[:, halves[hv]], ACTF.Sigmoid, scale=BIGS,
                                        bias=bias8[:, i:i + 1])
                    for hv in range(len(halves)):
                        p.wait("vector", "SD", sd_h[hv])
                        vd_h[hv] = v_op("scalar_tensor_tensor", uw[:, halves[hv]],
                                        bq[:, halves[hv]], -float(2 ** kb),
                                        src_u[:, halves[hv]], ALU.mult, ALU.add)
                    if full:
                        vd_h = [vd_h[0], vd_h[0]]
                else:
                    v_op("tensor_scalar", bq[:], src_u[:], float(2 ** kb), None,
                         ALU.is_ge, inc=None)
                    vdx = v_op("scalar_tensor_tensor", uw[:], bq[:], -float(2 ** kb),
                               src_u[:], ALU.mult, ALU.add)
                    vd_h = [vdx, vdx]
                first = False
                yield
            v_op("scalar_tensor_tensor", qs1[:], u0[:], -1.0, uw[:], ALU.mult, ALU.add,
                 inc=None)
            res[0] = v_op("tensor_scalar", out_sb[:], qs1[:], 1.0, 128.0,
                          ALU.mult, ALU.add)

        res = [0]

        def quant_bits(t, src_sb, out_sb=None, cb=0):
            if out_sb is None:
                out_sb = xq
            p.wait("vector", "TD", tr_state["last_td"])
            for _ in quant_bits_gen(t, src_sb, out_sb, cb):
                pass
            return res[0]

        def transpose_tile(src_sb, dest_fn, gamma_beta=False, src_ready_vd=None,
                           dest4_fn=None):
            if src_ready_vd is not None:
                p.wait("tensor", "VD", src_ready_vd)
            last_sd = 0
            for g4 in range(NB // 4):
                bank = g4 % 2
                p.wait("tensor", "SD", pT_last[bank])
                td = None
                for i in range(4):
                    ic = g4 * 4 + i
                    td = p.emit("tensor",
                                lambda e, o=pT[bank][:, 128 * i:128 * (i + 1)],
                                i_=src_sb[:, ic * 128:(ic + 1) * 128]:
                                e.transpose(o, i_, ident[:]),
                                inc=("TD" if i == 3 else None))
                tr_state["last_td"] = td
                p.wait("scalar", "TD", td)
                if dest4_fn is not None:
                    dst = dest4_fn(g4 * 4)
                else:
                    dst = dest_fn(g4 * 4)
                psrc = pT[bank][:, 0:512].rearrange("p (i c) -> p i c", c=128)
                if gamma_beta:
                    for i in range(4):
                        ic = g4 * 4 + i
                        sd = s_op("activation", dst[:, i, :], psrc[:, i, :],
                                  ACTF.Identity, bias=bet[:, ic:ic + 1],
                                  scale=gam[:, ic:ic + 1])
                else:
                    sd = s_op("copy", dst, psrc)
                pT_last[bank] = sd
                last_sd = sd
            return last_sd

        tok_c = [p.dma("gpsimd", ident[:], dident, "LD"),
                 p.dma("gpsimd", bdm[:], dbdm, "LD"),
                 p.dma("gpsimd", gam[:], dgam.rearrange("(c p) -> p c", p=128), "LD", slow=True),
                 p.dma("gpsimd", bet[:], dbet.rearrange("(c p) -> p c", p=128), "LD", slow=True)]
        p.wait_tok("vector", tok_c)
        v_op("memset", zer[:], 0.0)
        v_op("memset", eps5[:], 1e-5)
        v_op("memset", one1[:], 1.0)
        v_op("memset", accw[:], 0.0)
        v_op("memset", onesf[:], 1.0)
        v_op("memset", pA[:], 0.0)
        for i, kb in enumerate(range(7, -1, -1)):
            v_op("memset", bias8[:, i:i + 1], -BIGS * float(2 ** kb))
        vd_init = v_op("memset", anch[:, 0:1], 0.0)
        for eng in ("sync", "gpsimd", "tensor", "vector", "scalar"):
            p.wait(eng, "VD", vd_init)

        pbuf = [ost, qs1]
        lds = [p.dma("gpsimd", pbuf[r % 2][:], dfw[r * 128:(r + 1) * 128, :], "LD")
               for r in range(2)]
        vd = 0
        vd_red = {}
        for r in range(NB):
            p.wait_tok("vector", lds[r])
            vd_red[r] = v_op("tensor_reduce", st1[:, 0:1], pbuf[r % 2][:], AX.X, ALU.add,
                             apply_absolute_value=True)
            vd = v_op("tensor_tensor", accw[:], accw[:], st1[:, 0:1], ALU.add)
            if r + 2 < NB:
                p.wait("gpsimd", "VD", vd_red[r])
                lds.append(p.dma("gpsimd", pbuf[r % 2][:],
                                 dfw[(r + 2) * 128:(r + 3) * 128, :], "LD"))
        p.wait("tensor", "VD", vd)
        td = t_mm(pj[2][:, 0:1], onesf[:], accw[:], True, True, inc="TD")
        p.wait("vector", "TD", td)
        v_op("tensor_scalar", alph[:], pj[2][:, 0:1], 1.0 / (D * D), 1e-8,
             ALU.mult, ALU.max, inc=None)
        v_op("tensor_scalar", thrP[:], alph[:], 0.5, None, ALU.mult, inc=None)
        vd_alpha = v_op("tensor_scalar", thrN[:], alph[:], -0.5, None, ALU.mult)
        p.wait("scalar", "VD", vd_alpha)

        sf_toks = []
        stage = [ysl[0], ysl[1], ysl[2], ysl[3]]
        stage_tok = [[] for _ in range(4)]
        srcbuf = [ost, qs1]
        src_done = [("VD", 0), ("VD", 0)]
        tok_xq = []
        si = 0
        vd_tern = {}
        for r in range(NB):
            if r >= 2:
                p.wait("gpsimd", "VD", vd_tern[r - 2])
            else:
                p.wait("gpsimd", "VD", vd_red[14 + r])
            ld = p.dma("gpsimd", pbuf[r % 2][:], dfw[r * 128:(r + 1) * 128, :], "LD")
            p.wait_tok("vector", ld)
            tbuf = xq if r % 2 == 0 else nt
            p.wait_tok("vector", tok_xq[r % 2] if isinstance(tok_xq, dict) else None)
            if not isinstance(tok_xq, dict):
                tok_xq = {0: None, 1: None}
            p.wait_tok("vector", tok_xq[r % 2])
            v_op("tensor_scalar", bq[:], pbuf[r % 2][:], thrP[:], None, ALU.is_gt, inc=None)
            vd = v_op("scalar_tensor_tensor", tbuf[:], pbuf[r % 2][:], thrN[:], bq[:],
                      ALU.is_lt, ALU.subtract)
            vd_tern[r] = vd
            src_done[r % 2] = ("VD", vd)
            p.wait("sync", "VD", vd)
            tok_xq[r % 2] = p.dma("sync",
                                  s_f[:, r * 128:(r + 1) * 128, :].rearrange("g p c -> p g c"),
                                  tbuf[:].rearrange("p (g c) -> p g c", g=4), "WS")
            sf_toks.append(tok_xq[r % 2])
        xrows = dx.rearrange("(n p) d -> n p d", p=128)
        orows = dout.rearrange("(n p) d -> n p d", p=128)
        NSL = SEQ * NSLAB
        slabs = [x1A, x1B]

        def aq_slot(t):
            if t < 3:
                return headw[0][:, t * D:(t + 1) * D]
            return headw[1][:, 0:D]

        pref = {}
        od_slot_tok = {}
        last_c_vd = [0, 0]
        td_projlast_g = [0]

        def a_prefetch_gen(g):
            xsp = slabs[g % 2]
            lds = []
            for t in range(TPS):
                p.wait("gpsimd", "VD", last_c_vd[g % 2])
                lds.append(p.dma("gpsimd", xsp[t][:], xrows[g * TPS + t], "LD"))
            yield
            for t in range(TPS):
                tile_stats(t, xsp[t], ready_ld=lds[t], minmax=True, cb=4)
                yield
            stats_finish(with_beta=True, lo=4, hi=8)
            yield
            bits_vd = []
            for t in range(TPS):
                p.wait("vector", "TD", td_projlast_g[0])
                for _ in quant_bits_gen(t, xsp[t], aq_slot(t), cb=4):
                    yield
                bits_vd.append(res[0])
                yield
            pref[g] = bits_vd

        def pump(gen, n=1):
            if gen is None:
                return None
            for _ in range(n):
                if next(gen, "DONE") == "DONE":
                    return None
            return gen

        gen0 = a_prefetch_gen(0)
        while pump(gen0):
            pass
        src_done = [[("VD", p.cnt["VD"]), ("SD", p.cnt["SD"])],
                    [("VD", p.cnt["VD"]), ("SD", p.cnt["SD"])]]
        for r in range(NB):
            for mi, (msrc, mdst, scl, eng) in enumerate((
                    (dwv, s_v, None, "scalar"), (dwo, s_o, None, "scalar"),
                    (dwq, None, RK, "vector"), (dwk, None, None, "vector"),
                    (dwg, None, None, "vector"))):
                buf = stage[si % 4]
                sbuf_src = srcbuf[si % 2]
                p.wait_tok("gpsimd", [src_done[si % 2]])
                ld = p.dma("gpsimd", sbuf_src[:],
                           msrc[r * 128:(r + 1) * 128, :], "LD")
                p.wait_tok(eng, ld)
                p.wait_tok(eng, stage_tok[si % 4])
                stage_tok[si % 4] = []
                if eng == "scalar":
                    done = ("SD", s_op("copy", buf[:], sbuf_src[:]))
                elif scl is None:
                    done = ("VD", v_op("tensor_copy", buf[:], sbuf_src[:]))
                else:
                    done = ("VD", v_op("tensor_scalar", buf[:], sbuf_src[:], scl, None, ALU.mult))
                src_done[si % 2] = done
                p.wait_tok("sync", [done])
                if mdst is not None:
                    stage_tok[si % 4].append(
                        p.dma("sync",
                              mdst[:, r * 128:(r + 1) * 128, :].rearrange("g p c -> p g c"),
                              buf[:].rearrange("p (g c) -> p g c", g=4), "WS"))
                else:
                    j = mi - 2
                    stage_tok[si % 4].append(
                        p.dma("sync",
                              s_qkg[:, r * 128:(r + 1) * 128, j * HK:(j + 1) * HK]
                              .rearrange("h p c -> p h c"),
                              buf[:].rearrange("p (h c) -> p h c", h=H), "WS", slow=True))
                si += 1
        p.wait_tok("sync", sf_toks)
        p.wait_tok("vector", sf_toks)
        _gdb = p.pool_barrier_tok("WS")
        prep_done = {"armed": False}
        prep_stage_toks = [t for lst in stage_tok for t in lst]

        def load_piece(scratch, pg, buf):
            if scratch is not s_f and not prep_done["armed"]:
                prep_done["armed"] = True
                p.wait_tok("sync", _gdb)
                p.wait_tok("vector", _gdb)
                p.wait_tok("scalar", _gdb)
            og, half = pg // 2, pg % 2
            toks = []
            for part in range(2):
                toks.append(p.dma(
                    "sync",
                    fatw[buf][:].rearrange("p (ic c) -> p ic c", c=256)
                    [:, part * 8:(part + 1) * 8, :],
                    scratch[og, part * 1024:(part + 1) * 1024,
                            half * 256:(half + 1) * 256]
                    .rearrange("(ic p) c -> p ic c", p=128), "WS"))
            return toks

        def load_head(hh, buf):
            toks = []
            for part in range(2):
                toks.append(p.dma(
                    "sync",
                    headw[buf][:].rearrange("p (ic c) -> p ic c", c=3 * HK)
                    [:, part * 8:(part + 1) * 8, :],
                    s_qkg[hh, part * 1024:(part + 1) * 1024, :]
                    .rearrange("(ic p) c -> p ic c", p=128), "WS"))
            return toks

        def fat_phase(scratch, lhsT_fn, evac_fn, lhsT_ready_sd, pump_fn=None):
            if not isinstance(lhsT_ready_sd, list):
                lhsT_ready_sd = [lhsT_ready_sd] * TPS
            p.wait("sync", "TD", p.cnt["TD"])
            ws = [load_piece(scratch, 0, 0), load_piece(scratch, 1, 1)]
            evac_hist = {}
            td_piece = {}
            for pg in range(8):
                p.wait_tok("tensor", ws[pg])
                fb = fatw[pg % 2]
                for t in range(TPS):
                    p.wait("tensor", "SD", lhsT_ready_sd[t])
                    bank = pj[t % 2]
                    key = (pg * TPS + t) - 2
                    if key >= 0:
                        sname, sval = evac_hist[key]
                        p.wait("tensor", sname, sval)
                    td = None
                    for ic in range(NB):
                        td = t_mm(bank[:, 0:256], lhsT_fn(t, ic),
                                  fb[:, ic * 256:(ic + 1) * 256],
                                  ic == 0, ic == NB - 1,
                                  inc=("TD" if ic == NB - 1 else None))
                    evac_hist[pg * TPS + t] = evac_fn(t, pg, bank[:, 0:256], td)
                    if pump_fn is not None:
                        pump_fn()
                td_piece[pg] = td
                if pg + 2 < 8:
                    p.wait("sync", "TD", td_piece[pg])
                    ws.append(load_piece(scratch, pg + 2, pg % 2))
            for key in (8 * TPS - 2, 8 * TPS - 1):
                sname, sval = evac_hist[key]
                p.wait("tensor", sname, sval)
            return evac_hist

        def a_finish(gg, pump_fn=None):
            xsf = slabs[gg % 2]
            bits_vd = pref.pop(gg)
            trs = []
            for t in range(TPS):
                trs.append(transpose_tile(
                    aq_slot(t), None, src_ready_vd=bits_vd[t],
                    dest4_fn=lambda ic0, t=t: xqT[t][:, ic0 * 128:(ic0 + 4) * 128]
                    .rearrange("p (i c) -> p i c", c=128)))

            def evacA(t, pg, bank_ap, td):
                p.wait("vector", "TD", td)
                vd = v_op("scalar_tensor_tensor",
                          xsf[t][:, pg * 256:(pg + 1) * 256], bank_ap,
                          combos[:, 4 + t:5 + t],
                          xsf[t][:, pg * 256:(pg + 1) * 256], ALU.mult, ALU.add)
                return ("VD", vd)
            fat_phase(s_f, lambda t, ic: xqT[t][:, ic * 128:(ic + 1) * 128],
                      evacA, trs, pump_fn=pump_fn)

        for g in range(NSL):
            seq, sl = divmod(g, NSLAB)
            xs = slabs[g % 2]
            if sl == 0:
                vdS0 = v_op("memset", Ssb[:], 0.0)
                p.wait("tensor", "VD", vdS0)
                p.wait("vector", "VD", vdS0)

            if g == 0:
                a_finish(0, pump_fn=None)

            tr_sds = []
            for t in range(TPS):
                tile_stats(t, xs[t])
            stats_finish()
            for t in range(TPS):
                p.wait("vector", "TD", tr_state["last_td"])
                vd = v_op("tensor_scalar", nt[:], xs[t][:], mu[:, t:t + 1],
                          rstd[:, t:t + 1], ALU.subtract, ALU.mult)
                tr_sds.append(transpose_tile(
                    nt, None, src_ready_vd=vd,
                    dest4_fn=lambda ic0, t=t: xl2T[:]
                    .rearrange("p (ic s) -> p ic s", s=SLAB)
                    [:, ic0:ic0 + 4, t * 128:(t + 1) * 128]))

            def evacV(t, pg, bank_ap, td):
                p.wait("scalar", "TD", td)
                sd = s_op("copy", vsl[t][:, pg * 256:(pg + 1) * 256], bank_ap)
                return ("SD", sd)
            fat_phase(s_v,
                      lambda t, ic: xl2T[:, ic * SLAB + t * 128: ic * SLAB + (t + 1) * 128],
                      evacV, tr_sds)

            def qt_slot(h):
                return xqT[h // 4][:, (h % 4) * 512:(h % 4 + 1) * 512]

            def khT_slot(h, t):
                base2 = (h % 8) * 512 + t * 128
                return fatw[h // 8][:, base2:base2 + 128]

            p.wait("sync", "TD", p.cnt["TD"])
            ws_h = load_head(0, 0)

            def proj_qkg(h, gate_sd, q_sd, k_sd):
                hw = headw[h % 2]
                tds = {}
                for j, bank, wsd in ((2, pj[2], gate_sd),
                                     (0, pj[0], q_sd), (1, pj[1], k_sd)):
                    if wsd:
                        p.wait("tensor", "SD", wsd)
                    for ic in range(NB):
                        td = t_mm(bank[:],
                                  hw[:, ic * 384 + j * 128: ic * 384 + (j + 1) * 128],
                                  xl2T[:, ic * SLAB:(ic + 1) * SLAB],
                                  ic == 0, ic == NB - 1,
                                  inc=("TD" if ic == NB - 1 else None))
                    tds[j] = td
                return tds

            p.wait_tok("tensor", ws_h)
            if g == 0:
                p.wait_tok("scalar", prep_stage_toks)
                p.wait_tok("vector", prep_stage_toks)
            tds = proj_qkg(0, 0, None, None)
            td_kreaders = 0
            td_projlast = {-1: 0, 0: tds[1]}
            yevac_sd = 0
            for h in range(H):
                p.wait("scalar", "TD", tds[2])
                p.wait("scalar", "VD", p.cnt["VD"])
                sd_sig = s_op("copy", sg[:], pj[2][:])
                p.wait("scalar", "TD", tds[1])
                sd_q = s_op("copy", qr[:], pj[0][:])
                sd_k = s_op("copy", kr[:], pj[1][:])
                s_op("activation", gb[:], sg[:], ACTF.Exp)
                sd = s_op("activation", gb[:], gb[:], ACTF.Ln, bias=one1[:])
                p.wait("vector", "SD", sd)
                v_op("tensor_tensor", sg[:], sg[:], gb[:], ALU.subtract, inc=None)
                vd = v_op("tensor_tensor_scan", gb[:], sg[:], zer[:], 0.0,
                          ALU.add, ALU.add)
                gb3 = gb[:].rearrange("p (c j) -> p c j", j=C)
                v_op("tensor_copy", anch[:, 1:NCH].unsqueeze(2),
                     gb3[:, 0:NCH - 1, C - 1:C], inc=None)
                anb = anch[:].unsqueeze(2).broadcast_to([128, NCH, C])
                Dt3 = Dt[:].rearrange("p (c j) -> p c j", j=C)
                vd = v_op("tensor_tensor", Dt3, gb3, anb, ALU.subtract)
                p.wait("scalar", "VD", vd)
                sd_eq = s_op("activation", e_q[:], Dt[:], ACTF.Exp)
                sd_ek = s_op("activation", e_k[:], Dt[:], ACTF.Exp, scale=-1.0)
                Dend = Dt[:].rearrange("p (c j) -> p c j", j=C)[:, :, C - 1:C]
                sd_lam = s_op("activation",
                              lamf[:, h * NCH:(h + 1) * NCH].unsqueeze(2),
                              Dend, ACTF.Exp)
                p.wait("vector", "SD", sd_eq)
                q_vd = v_op("tensor_tensor", qt_slot(h), qr[:], e_q[:], ALU.mult)
                p.wait("vector", "SD", sd_ek)
                p.wait("vector", "TD", td_kreaders)
                k_vd = v_op("tensor_tensor", kt[:], kr[:], e_k[:], ALU.mult)
                lam3 = e_q[:].rearrange("p (c j) -> p c j", j=C)[:, :, C - 1:C] \
                    .broadcast_to([128, NCH, C])
                kt3 = kt[:].rearrange("p (c j) -> p c j", j=C)
                kh3 = kh[:].rearrange("p (c j) -> p c j", j=C)
                vd_kh = v_op("tensor_tensor", kh3, kt3, lam3, ALU.mult)
                p.wait("vector", "SD", sd_lam)
                vd_q = v_op("tensor_scalar", st2[:, 0:1], st1[:, 0:1], 1.0, None,
                            ALU.mult)
                if h + 1 < H:
                    p.wait("sync", "TD", td_projlast[h - 1])
                    ws_h = load_head(h + 1, (h + 1) % 2)
                    p.wait_tok("tensor", ws_h)
                    tds = proj_qkg(h + 1, sd_sig, sd_q, sd_k)
                    td_projlast[h + 1] = tds[1]
                p.wait("tensor", "VD", vd_q)
                bank = h % 2
                p.wait("tensor", "SD", pT_last[bank])
                for t in range(TPS):
                    td = p.emit("tensor",
                                lambda e, o=pT[bank][:, 128 * t:128 * (t + 1)],
                                i_=kh[:, t * 128:(t + 1) * 128]:
                                e.transpose(o, i_, ident[:]),
                                inc=("TD" if t == TPS - 1 else None))
                tr_state["last_td"] = td
                p.wait("scalar", "TD", td)
                sd_kh4 = s_op("copy",
                              fatw[h // 8][:, (h % 8) * 512:(h % 8 + 1) * 512]
                              .rearrange("p (i c) -> p i c", c=128),
                              pT[bank][:, 0:512].rearrange("p (i c) -> p i c", c=128))
                pT_last[bank] = sd_kh4
                for t in range(TPS):
                    tdA = t_mm(pA[:, 128 * t:128 * (t + 1)],
                               kt[:, t * 128:(t + 1) * 128],
                               qt_slot(h)[:, t * 128:(t + 1) * 128],
                               True, True, inc=("TD" if t == TPS - 1 else None))
                p.wait("vector", "TD", tdA)
                bdm4 = bdm[:].unsqueeze(1).broadcast_to([128, TPS, 128])
                vdA = v_op("tensor_tensor",
                           Asb[:].rearrange("p (t c) -> p t c", c=128),
                           pA[:, 0:512].rearrange("p (t c) -> p t c", c=128),
                           bdm4, ALU.mult)
                p.wait("tensor", "VD", vdA)
                p.wait("tensor", "SD", yevac_sd)
                for t in range(TPS):
                    p.wait("tensor", "SD", sd_kh4)
                    td_y = t_mm(pY[:, 128 * t:128 * (t + 1)],
                                Asb[:, 128 * t:128 * (t + 1)],
                                vsl[t][:, h * HK:(h + 1) * HK],
                                True, True, inc=("TD" if t == TPS - 1 else None))
                p.wait("scalar", "TD", td_y)
                ydst = ysl_all[:].rearrange("p (t d) -> p t d", d=D)[:, :, h * HK:(h + 1) * HK]
                yevac_sd = s_op("copy", ydst,
                                pY[:, 0:512].rearrange("p (t c) -> p t c", c=128))
                td_kreaders = p.cnt["TD"]
            td_projlast_g[0] = td_projlast[H - 1]

            agen = a_prefetch_gen(g + 1) if g + 1 < NSL else None
            YB = [pj[2], pA, pY, pD]
            sd_stage1 = p.cnt["SD"]
            p.wait("tensor", "SD", sd_stage1)
            p.wait("tensor", "VD", p.cnt["VD"])
            vd_evac = [p.cnt["VD"]] * 4
            vdS_hist = {}
            for t in range(TPS):
                for j in range(TPS):
                    c = t * TPS + j
                    for hg in range(4):
                        bank = pj[hg % 2]
                        for hi in range(4):
                            h = hg * 4 + hi
                            gidx, gi = h // 4, h % 4
                            if c > 0:
                                p.wait("tensor", "VD", vdS_hist[(c - 1, h)])
                            if j == 0:
                                p.wait("tensor", "VD", vd_evac[gidx])
                            t_mm(YB[gidx][C * j:C * (j + 1), gi * 128:(gi + 1) * 128],
                                 qt_slot(h)[:, t * 128 + j * C: t * 128 + j * C + C],
                                 Ssb[:, h * HK:(h + 1) * HK],
                                 True, True, tp=(0, C * j), skip=True)
                        prevg = (c, hg - 2) if hg >= 2 else (c - 1, hg + 2)
                        if prevg in vdS_hist:
                            p.wait("tensor", "VD", vdS_hist[prevg])
                        tdS = None
                        for hi in range(4):
                            h = hg * 4 + hi
                            tdS = t_mm(bank[:, hi * 128:(hi + 1) * 128],
                                       khT_slot(h, t)[C * j:C * (j + 1), :],
                                       vsl[t][C * j:C * (j + 1), h * HK:(h + 1) * HK],
                                       True, True, tp=(C * j, 0),
                                       inc=("TD" if hi == 3 else None))
                        p.wait("vector", "TD", tdS)
                        S4 = Ssb[:, hg * 512:(hg + 1) * 512] \
                            .rearrange("p (i c2) -> p i c2", c2=128)
                        lam4 = lamf[:, hg * 4 * NCH + c: (hg * 4 + 3) * NCH + c + 1: NCH] \
                            .unsqueeze(2).broadcast_to([128, 4, 128])
                        v_op("tensor_tensor", S4, S4, lam4, ALU.mult, inc=None)
                        vd_s = v_op("tensor_tensor", S4, S4,
                                    bank[:, 0:512].rearrange("p (i c2) -> p i c2", c2=128),
                                    ALU.add)
                        for hi in range(4):
                            vdS_hist[(c, hg * 4 + hi)] = vd_s
                        vdS_hist[(c, hg)] = vd_s
                        agen = pump(agen)
                p.wait("vector", "TD", p.cnt["TD"])
                p.wait("vector", "SD", sd_stage1)
                for gg in range(4):
                    ysl_slice = ysl_all[:, t * D + gg * 512: t * D + (gg + 1) * 512]
                    vd_evac[gg] = v_op("scalar_tensor_tensor",
                                       ysl_slice, YB[gg][:], 1.0,
                                       ysl_slice, ALU.mult, ALU.add)
                tile_stats(t, ysl_all[:, t * D:(t + 1) * D], dump=nt)
            while pump(agen):
                pass

            tr_sds = []
            stats_finish()
            for t in range(TPS):
                p.wait("vector", "TD", tr_state["last_td"])
                vd = v_op("tensor_scalar", nt[:], ysl_all[:, t * D:(t + 1) * D],
                          mu[:, t:t + 1], rstd[:, t:t + 1], ALU.subtract, ALU.mult)
                tr_sds.append(transpose_tile(
                    nt, None, gamma_beta=True, src_ready_vd=vd,
                    dest4_fn=lambda ic0, t=t: xqT[t][:, ic0 * 128:(ic0 + 4) * 128]
                    .rearrange("p (i c) -> p i c", c=128)))

            def evacO(t, pg, bank_ap, td):
                p.wait("vector", "TD", td)
                vd = v_op("scalar_tensor_tensor",
                          xs[t][:, pg * 256:(pg + 1) * 256], bank_ap, 1.0,
                          xs[t][:, pg * 256:(pg + 1) * 256], ALU.mult, ALU.add)
                return ("VD", vd)
            fat_phase(s_o, lambda t, ic: xqT[t][:, ic * 128:(ic + 1) * 128],
                      evacO, tr_sds)

            for t in range(TPS):
                tile_stats(t, xs[t], minmax=True)
            stats_finish(with_beta=True)

            cgens = [None]

            def c_steps():
                cvds = []
                for t in range(TPS):
                    p.wait("vector", "TD", tr_state["last_td"])
                    for _ in quant_bits_gen(t, xs[t], vsl[t]):
                        yield
                    cvds.append(res[0])
                    yield
                cgens[0] = cvds

            cg = c_steps()

            def cpump():
                next(cg, "DONE")
            if g + 1 < NSL:
                a_finish(g + 1, pump_fn=cpump)
            while next(cg, "DONE") != "DONE":
                pass
            cvds = cgens[0]
            tr_sds = []
            for t in range(TPS):
                p.wait("scalar", "TD", p.cnt["TD"])
                tr_sds.append(transpose_tile(
                    vsl[t], None, src_ready_vd=cvds[t],
                    dest4_fn=lambda ic0, t=t: xqT[t][:, ic0 * 128:(ic0 + 4) * 128]
                    .rearrange("p (i c) -> p i c", c=128)))

            ost8 = ost[:].rearrange("p (s c) -> p s c", c=256)

            def evacC(t, pg, bank_ap, td):
                slot = (pg * TPS + t) % 8
                p.wait("vector", "TD", td)
                p.wait_tok("vector", od_slot_tok.get(slot))
                vd = v_op("scalar_tensor_tensor",
                          ost8[:, slot, :], bank_ap, combos[:, t:t + 1],
                          xs[t][:, pg * 256:(pg + 1) * 256], ALU.mult, ALU.add)
                p.wait("gpsimd", "VD", vd)
                od_slot_tok[slot] = p.dma(
                    "gpsimd", orows[g * TPS + t][:, pg * 256:(pg + 1) * 256],
                    ost8[:, slot, :], "OD")
                return ("VD", vd)
            hist = fat_phase(s_f, lambda t, ic: xqT[t][:, ic * 128:(ic + 1) * 128],
                             evacC, tr_sds)
            last_c_vd[g % 2] = max(hist[7 * TPS + t][1] for t in range(TPS))

        p.wait_tok("sync", p.pool_barrier_tok("OD"))

        @blk.sync
        def _(e):
            for f in p.q["sync"]:
                f(e)

        @blk.gpsimd
        def _(e):
            for f in p.q["gpsimd"]:
                f(e)

        @blk.tensor
        def _(e):
            for f in p.q["tensor"]:
                f(e)

        @blk.vector
        def _(e):
            for f in p.q["vector"]:
                f(e)

        @blk.scalar
        def _(e):
            for f in p.q["scalar"]:
                f(e)

    return nc


_NC_CACHE = None


def kernel(L0_attn_x, L0_attn_Wq, L0_attn_Wk, L0_attn_Wv, L0_attn_Wg, L0_attn_Wo,
           L0_attn_gamma, L0_attn_beta, L0_ffn_weight):
    global _NC_CACHE
    if _NC_CACHE is None:
        _NC_CACHE = build_program()
    nc = _NC_CACHE

    x = np.ascontiguousarray(np.asarray(L0_attn_x, dtype=np.float32))
    wq = np.ascontiguousarray(np.asarray(L0_attn_Wq, np.float32).T)
    wk = np.ascontiguousarray(np.asarray(L0_attn_Wk, np.float32).T)
    wv = np.ascontiguousarray(np.asarray(L0_attn_Wv, np.float32).T)
    wg = np.ascontiguousarray(np.asarray(L0_attn_Wg, np.float32).T)
    wo = np.ascontiguousarray(np.asarray(L0_attn_Wo, np.float32).T)
    fw = np.ascontiguousarray(np.asarray(L0_ffn_weight, np.float32).T)
    gamma = np.ascontiguousarray(np.asarray(L0_attn_gamma, np.float32))
    beta = np.ascontiguousarray(np.asarray(L0_attn_beta, np.float32))
    ident = np.eye(128).astype(ml_dtypes.bfloat16)
    s_i = np.arange(128)[:, None]
    t_i = np.arange(128)[None, :]
    bdmask = (((s_i // C) == (t_i // C)) & (s_i <= t_i)).astype(ml_dtypes.bfloat16)

    in_maps = []
    for c in range(NCORES):
        shard = np.ascontiguousarray(x[2 * c:2 * c + 2].reshape(SEQ * L, D))
        in_maps.append({
            "x": shard, "wq": wq, "wk": wk, "wv": wv, "wg": wg, "wo": wo,
            "fw": fw, "gamma": gamma, "beta": beta,
            "ident": ident, "bdmask": bdmask,
        })
    res = run_bass_kernel_spmd(nc, in_maps, core_ids=list(range(NCORES)))
    out = np.empty((16, L, D), np.float32)
    for c in range(NCORES):
        out[2 * c:2 * c + 2] = np.asarray(res.results[c]["out"]).reshape(SEQ, L, D)
    return out
